# version 05
# baseline (speedup 1.0000x reference)
import numpy as np
import concourse.bass as bass
import concourse.mybir as mybir
from concourse.bass_utils import run_bass_kernel_spmd

F32 = mybir.dt.float32
BF16 = mybir.dt.bfloat16
ALU = mybir.AluOpType
ACT = mybir.ActivationFunctionType

B, T, D, H = 4, 4096, 1024, 16
TOWN = 2048
NBIS = 38
GRP = 4
DC = D // 128
QT = TOWN // 128
JC = T // 128
CNT_THR = 408.5


def build_nc(stop_phase=99, dbg=None, hset=None, hrun=None):
    nc = bass.Bass("TRN2", target_bir_lowering=False,
                   detect_race_conditions=True,
                   dynamic_dma_scratch_size=2048)
    xq = nc.dram_tensor("xq", [D, TOWN], F32, kind="ExternalInput").ap()
    xfull = nc.dram_tensor("xfull", [D, T], F32, kind="ExternalInput").ap()
    wqkvT = nc.dram_tensor("wqkvT", [D, 3 * D], F32, kind="ExternalInput").ap()
    woutT = nc.dram_tensor("woutT", [D, D], F32, kind="ExternalInput").ap()
    boutd = nc.dram_tensor("boutd", [DC, 128], F32, kind="ExternalInput").ap()
    w1d = nc.dram_tensor("w1d", [D, D], F32, kind="ExternalInput").ap()
    w2d = nc.dram_tensor("w2d", [D, D], F32, kind="ExternalInput").ap()
    w3d = nc.dram_tensor("w3d", [D, D], F32, kind="ExternalInput").ap()
    fbiasd = nc.dram_tensor("fbiasd", [DC, 128], F32, kind="ExternalInput").ap()
    eyed = nc.dram_tensor("eyed", [128, 128], F32, kind="ExternalInput").ap()
    vaug = nc.dram_tensor("vaug", [H, 128, JC, 65], BF16, kind="Internal").ap()
    out = nc.dram_tensor("out", [TOWN, D], F32, kind="ExternalOutput").ap()

    sems = {}
    def S(name):
        if name not in sems:
            ctx = nc.semaphore(name)
            h = ctx.__enter__()
            sems[name] = [ctx, 0, h]
        return sems[name][2]
    def bump(name, k=1):
        S(name)
        sems[name][1] += k
        return sems[name][1]
    def cur(name):
        S(name)
        return sems[name][1]

    live = {}
    def alloc(name, shape, dt):
        t = nc.sbuf_tensor(name, shape, dt)
        h = t.__enter__()
        live[name] = t
        return h
    def free(*names):
        for n in names:
            live.pop(n).__exit__(None, None, None)

    def palloc(name, shape, dt):
        t = nc.psum_tensor(name, shape, dt)
        h = t.__enter__()
        return h

    pa = palloc("pa", [128, 1024], F32)
    pb = palloc("pb", [128, 1024], F32)
    ptp = palloc("ptp", [128, 2048], BF16)
    ppv = palloc("ppv", [65, 512], F32)
    pm = palloc("pm", [128, 512], F32)

    eye32 = alloc("eye32", [128, 128], F32)
    eye16 = alloc("eye16", [128, 128], BF16)
    ones1 = alloc("ones1", [1, 128], F32)
    onesc = alloc("onesc", [128, 1], F32)
    tx = alloc("tx", [128, TOWN], F32)
    wf = alloc("wf", [128, 1024], F32)
    w16 = [alloc("w16a", [128, 1024], BF16), alloc("w16b", [128, 1024], BF16)]
    biasb = alloc("biasb", [128, DC], F32)
    fbb = alloc("fbb", [128, DC], F32)

    qts = [alloc("q16_%d" % i, [128, TOWN], BF16) for i in range(DC)]
    aots = [alloc("ao16_%d" % i, [128, TOWN], BF16) for i in range(DC)]

    def emit_load_bias(sy, dst, src):
        for dc in range(DC):
            sy.dma_start(out=dst[:, dc:dc + 1],
                         in_=src[dc, :]).then_inc(S("ld"), 16)
            bump("ld", 16)

    def emit_proj(wsrc, wcol0, n_ec, mov, n_half, dst_write, tag,
                  evac="scalar", act=ACT.Copy, scale=1.0, bias_col=None):
        bw = cur("wld"); bo = cur("wok")
        bm = cur("mm"); be = cur("ev")
        with nc.Block() as blk:
            @blk.sync
            def _(sy):
                for i in range(n_ec):
                    if i >= 1:
                        sy.wait_ge(S("wok"), bo + i)
                    for dcc in range(DC):
                        sy.dma_start(
                            out=wf[:, dcc * 128:(dcc + 1) * 128],
                            in_=wsrc[dcc * 128:(dcc + 1) * 128,
                                     wcol0 + i * 128:wcol0 + (i + 1) * 128],
                        ).then_inc(S("wld"), 16)
                    bump("wld", 16 * DC)

            @blk.vector
            def _(v):
                for i in range(n_ec):
                    v.wait_ge(S("wld"), bw + 16 * DC * (i + 1))
                    if i >= 2:
                        v.wait_ge(S("mm"), bm + (i - 1) * n_half)
                    v.tensor_copy(w16[i % 2][:, :], wf[:, :]).then_inc(S("wok"), 1)
                    bump("wok")

            @blk.tensor
            def _(te):
                m = 0
                for i in range(n_ec):
                    te.wait_ge(S("wok"), bo + i + 1)
                    for half in range(n_half):
                        psu = pa if m % 2 == 0 else pb
                        if m >= 2:
                            te.wait_ge(S("ev"), be + m - 1)
                        for dcc in range(DC):
                            for f in range(2):
                                c0 = half * 1024 + f * 512
                                inst = te.matmul(psu[:, f * 512:(f + 1) * 512],
                                                 w16[i % 2][:, dcc * 128:(dcc + 1) * 128],
                                                 mov(dcc, c0),
                                                 start=(dcc == 0), stop=(dcc == DC - 1))
                                if dcc == DC - 1 and f == 1:
                                    inst.then_inc(S("mm"), 1)
                        bump("mm")
                        m += 1

            @blk.scalar
            def _(sc):
                m = 0
                for i in range(n_ec):
                    for half in range(n_half):
                        psu = pa if m % 2 == 0 else pb
                        sc.wait_ge(S("mm"), bm + m + 1)
                        if bias_col is None:
                            sc.activation(out=dst_write(i, half),
                                          in_=psu[:, :], func=act,
                                          scale=scale).then_inc(S("ev"), 1)
                        else:
                            sc.activation(out=dst_write(i, half),
                                          in_=psu[:, :], func=act,
                                          scale=scale,
                                          bias=bias_col(i)).then_inc(S("ev"), 1)
                        bump("ev")
                        m += 1

    xq16 = alloc("xq16", [128, DC, TOWN], BF16)
    with nc.Block() as blk:
        @blk.sync
        def _(sy):
            sy.dma_start(out=eye32[:, :], in_=eyed[:, :]).then_inc(S("ld"), 16)
            bump("ld", 16)
            emit_load_bias(sy, biasb, boutd)
            emit_load_bias(sy, fbb, fbiasd)
        @blk.vector
        def _(v):
            v.wait_ge(S("ld"), cur("ld"))
            v.tensor_copy(eye16[:, :], eye32[:, :])
            v.memset(ones1[:, :], 1.0)
            v.memset(onesc[:, :], 1.0)

    for dc in range(DC):
        with nc.Block() as blk:
            @blk.sync
            def _(sy, dc=dc):
                sy.dma_start(out=tx[:, 0:TOWN],
                             in_=xq[dc * 128:(dc + 1) * 128, :]).then_inc(S("lx"), 16)
                bump("lx", 16)
            @blk.vector
            def _(v, dc=dc):
                v.wait_ge(S("lx"), cur("lx"))
                v.tensor_copy(xq16[:, dc, :], tx[:, 0:TOWN])

    emit_proj(wqkvT, 0, 8,
              lambda dcc, c0: xq16[:, dcc, c0:c0 + 512], 2,
              lambda ec, half: qts[ec][:, half * 1024:(half + 1) * 1024],
              "q", evac="scalar", act=ACT.Copy, scale=0.03125)
    free("xq16")
    if stop_phase <= 0:
        return _finish_stub(nc, out, q16, S, bump, cur, live, sems)

    kts = [alloc("k16_%d" % i, [128, T], BF16) for i in range(DC)]
    x16 = alloc("x16", [128, DC, T], BF16)
    for dc in range(DC):
        for hh in range(2):
            with nc.Block() as blk:
                @blk.sync
                def _(sy, dc=dc, hh=hh):
                    sy.dma_start(out=tx[:, :],
                                 in_=xfull[dc * 128:(dc + 1) * 128,
                                           hh * TOWN:(hh + 1) * TOWN]
                                 ).then_inc(S("lx2"), 16)
                    bump("lx2", 16)
                @blk.vector
                def _(v, dc=dc, hh=hh):
                    v.wait_ge(S("lx2"), cur("lx2"))
                    v.tensor_copy(x16[:, dc, hh * TOWN:(hh + 1) * TOWN], tx[:, :])

    emit_proj(wqkvT, D, 8,
              lambda dcc, c0: x16[:, dcc, c0:c0 + 512], 4,
              lambda ec, half: kts[ec][:, half * 1024:(half + 1) * 1024],
              "k", evac="scalar", act=ACT.Copy, scale=1.0)

    v16 = alloc("v16", [128, T], BF16)
    vtt = alloc("vtt", [128, JC, 65], BF16)
    for ec in range(8):
        emit_proj(wqkvT, 2 * D + ec * 128 - 0, 1,
                  lambda dcc, c0: x16[:, dcc, c0:c0 + 512], 4,
                  lambda _i, half: v16[:, half * 1024:(half + 1) * 1024],
                  "v%d" % ec, evac="scalar", act=ACT.Relu, scale=1.0)
        for hp in range(2):
            h = ec * 2 + hp
            with nc.Block() as blk:
                @blk.tensor
                def _(te, hp=hp):
                    for blkj in range(JC):
                        inst = te.transpose(ptp[:, blkj * 64:(blkj + 1) * 64],
                                            v16[hp * 64:(hp + 1) * 64,
                                                blkj * 128:(blkj + 1) * 128],
                                            eye16[hp * 64:(hp + 1) * 64,
                                                  hp * 64:(hp + 1) * 64])
                        if blkj == JC - 1:
                            inst.then_inc(S("vtr"), 1)
                    bump("vtr")
                @blk.vector
                def _(v, h=h):
                    v.wait_ge(S("vtr"), cur("vtr"))
                    v.wait_ge(S("vsp"), 16 * h)
                    v.tensor_copy(vtt[:, :, 0:64],
                                  ptp[:, :].rearrange("p (a b) -> p a b", b=64))
                    v.memset(vtt[:, :, 64:65], 1.0).then_inc(S("vttok"), 1)
                    bump("vttok")
                @blk.sync
                def _(sy, h=h):
                    sy.wait_ge(S("vttok"), cur("vttok"))
                    sy.dma_start(out=vaug[h], in_=vtt[:, :, :]).then_inc(S("vsp"), 16)
                    bump("vsp", 16)
    free("vtt", "v16", "x16")
    if stop_phase <= 1:
        return _finish_stub(nc, out, k16, S, bump, cur, live, sems)

    sg = alloc("sg", [128, GRP, T], BF16)
    et = alloc("et", [128, T], BF16)
    ptr = alloc("ptr", [128, JC * 128], BF16)
    pflat = ptr
    va01 = [alloc("va0", [128, JC * 65], BF16), alloc("va1", [128, JC * 65], BF16)]
    zrow = alloc("zrow", [65, 512], F32)
    zrow2 = alloc("zrow2", [65, 512], F32)
    ones65 = alloc("ones65", [65, 128], F32)
    aotmp = alloc("aotmp", [64, 512], BF16)
    zb = alloc("zb", [64, 512], F32)
    lo = alloc("lo", [128, GRP], F32)
    hi = alloc("hi", [128, GRP], F32)
    mid = alloc("mid", [128, GRP], F32)
    cnt = alloc("cnt", [128, GRP], F32)
    g01 = alloc("g01", [128, GRP], F32)
    tt1 = alloc("tt1", [128, GRP], F32)
    tt2 = alloc("tt2", [128, GRP], F32)
    tt3 = alloc("tt3", [128, GRP], F32)
    tt4 = alloc("tt4", [128, GRP], F32)
    neglo = alloc("neglo", [128, GRP], F32)


    def _dbg_dump(aps):
        with nc.Block() as blk:
            @blk.vector
            def _(v):
                for i, (ap, _) in enumerate(aps):
                    v.wait_ge(S("st"), 16 * i)
                    v.tensor_copy(tx[0:ap.shape[0], 0:ap.shape[-1]], ap).then_inc(S("stub"), 1)
                    bump("stub")
            @blk.sync
            def _(sy):
                for i, (ap, _) in enumerate(aps):
                    sy.wait_ge(S("stub"), i + 1)
                    sy.dma_start(out=out[i * 128:i * 128 + ap.shape[0], 0:ap.shape[-1]],
                                 in_=tx[0:ap.shape[0], 0:ap.shape[-1]]).then_inc(S("st"), 16)
                    bump("st", 16)
                sy.wait_ge(S("st"), cur("st"))
        for n in reversed(list(live)):
            live.pop(n).__exit__(None, None, None)
        for name in list(sems):
            sems.pop(name)[0].__exit__(None, None, None)
        return nc

    with nc.Block() as blk:
        @blk.sync
        def _(sy):
            sy.wait_ge(S("vsp"), cur("vsp"))
            for hh in (hset[:2] if hset else [0]):
                sy.dma_start(out=va01[hh % 2].rearrange("p (a b) -> p a b", b=65),
                             in_=vaug[hh]).then_inc(S("val"), 16)
                bump("val", 16)
        @blk.vector
        def _(v):
            v.memset(ones65[:, :], 1.0)

    with nc.Block() as blk:
        @blk.tensor
        def _(te):
            te.wait_ge(S("ev"), cur("ev"))

    HRUN = hrun if hrun else (2 if dbg == "kchk" else (3 if dbg == "h3" else H))
    for h in (hset if hset else range(HRUN)):
        hc, hp = h // 2, (h % 2) * 64
        if hset is None and h + 1 < H:
            with nc.Block() as blk:
                @blk.sync
                def _(sy, h=h, b_pvz=cur("pvz")):
                    sy.wait_ge(S("pvz"), b_pvz)
                    sy.dma_start(out=va01[(h + 1) % 2].rearrange("p (a b) -> p a b", b=65),
                                 in_=vaug[h + 1]).then_inc(S("val"), 16)
                    bump("val", 16)
        for g in range(QT // GRP):
            base_mm = cur("smm")
            base_ev = cur("sev")
            with nc.Block() as blk:
                @blk.tensor
                def _(te, h=h, g=g, hc=hc, hp=hp, base_ev=base_ev):
                    te.wait_ge(S("sev"), base_ev)
                    m = 0
                    for t in range(GRP):
                        qt = g * GRP + t
                        for quad in range(4):
                            psu = pa if m % 2 == 0 else pb
                            if m >= 2:
                                te.wait_ge(S("sev"), base_ev + m - 1)
                            for f in range(2):
                                j0 = quad * 1024 + f * 512
                                inst = te.matmul(psu[:, f * 512:(f + 1) * 512],
                                                 qts[hc][hp:hp + 64,
                                                         qt * 128:(qt + 1) * 128],
                                                 kts[hc][hp:hp + 64, j0:j0 + 512],
                                                 start=True, stop=True)
                                if f == 1:
                                    inst.then_inc(S("smm"), 1)
                            bump("smm")
                            m += 1
                @blk.scalar
                def _(sc, base_mm=base_mm, b_pokg=cur("pok")):
                    sc.wait_ge(S("pok"), b_pokg)
                    m = 0
                    for t in range(GRP):
                        for quad in range(4):
                            psu = pa if m % 2 == 0 else pb
                            sc.wait_ge(S("smm"), base_mm + m + 1)
                            sc.activation(out=sg[:, t, quad * 1024:(quad + 1) * 1024],
                                          in_=psu[:, :], func=ACT.Copy,
                                          scale=1.0).then_inc(S("sev"), 1)
                            bump("sev")
                            m += 1

            if dbg is not None and dbg.startswith("sg") and h == int(dbg[2:] or 0) and g == 0:
                return _dbg_dump([(sg[:, 0, i * 1024:(i + 1) * 1024], True)
                                  for i in range(4)])
            with nc.Block() as blk:
                @blk.vector
                def _(v, b_sev=cur("sev"), b_eokg=cur("eok"), b_pvzg=cur("pvz")):
                    v.wait_ge(S("sev"), b_sev)
                    v.wait_ge(S("eok"), b_eokg)
                    v.wait_ge(S("pvz"), b_pvzg)
                    v.memset(lo[:, :], 0.0)
                    v.memset(hi[:, :], 8.0)
                    v.memset(mid[:, :], 4.0)
                    v.drain()
                    for it in range(NBIS):
                        for t in range(GRP):
                            v.tensor_scalar(et[:, :] if t % 2 == 0 else pflat[:, :],
                                            sg[:, t, :],
                                            mid[:, t:t + 1], 0.0,
                                            op0=ALU.is_ge, op1=ALU.add,
                                            accum_out=cnt[:, t:t + 1])
                            if t % 2 == 1 or t == GRP - 1:
                                v.drain()
                        v.tensor_scalar(g01[:, :], cnt[:, :], CNT_THR, None,
                                        op0=ALU.is_gt)
                        v.tensor_tensor(out=tt1[:, :], in0=mid[:, :],
                                        in1=lo[:, :], op=ALU.subtract)
                        v.tensor_tensor(out=tt3[:, :], in0=hi[:, :],
                                        in1=mid[:, :], op=ALU.subtract)
                        v.drain()
                        v.tensor_tensor(out=tt2[:, :], in0=g01[:, :],
                                        in1=tt1[:, :], op=ALU.mult)
                        v.tensor_tensor(out=tt4[:, :], in0=g01[:, :],
                                        in1=tt3[:, :], op=ALU.mult)
                        v.drain()
                        v.tensor_tensor(out=lo[:, :], in0=lo[:, :],
                                        in1=tt2[:, :], op=ALU.add)
                        v.tensor_tensor(out=hi[:, :], in0=mid[:, :],
                                        in1=tt4[:, :], op=ALU.add)
                        v.drain()
                        v.tensor_tensor(out=tt1[:, :], in0=lo[:, :],
                                        in1=hi[:, :], op=ALU.add)
                        v.drain()
                        v.tensor_scalar(mid[:, :], tt1[:, :], 0.5, None,
                                        op0=ALU.mult)
                        v.drain()
                    v.tensor_scalar(neglo[:, :], lo[:, :], -1.0, None,
                                    op0=ALU.mult).then_inc(S("bis"), 1)
                    bump("bis")
                    v.drain()

            if dbg is not None and dbg.startswith("bis") and h == int(dbg[3:] or 0) and g == 0:
                return _dbg_dump([(lo[:, :], False), (hi[:, :], False),
                                  (cnt[:, :], False), (neglo[:, :], False)])
            for t in range(GRP):
                qt = g * GRP + t
                b_eok = cur("eok"); b_pok = cur("pok")
                b_ptm = cur("ptm"); b_pte = cur("pte")
                b_pvz = cur("pvz"); b_bis = cur("bis"); b_aosc = cur("aosc")
                with nc.Block() as blk:
                    @blk.scalar
                    def _(sc, t=t, b_pok=b_pok, b_bis=b_bis):
                        sc.wait_ge(S("bis"), b_bis)
                        sc.wait_ge(S("pok"), b_pok)
                        sc.activation(out=et[:, :], in_=sg[:, t, :],
                                      func=ACT.Exp, bias=neglo[:, t:t + 1],
                                      scale=1.0).then_inc(S("eok"), 1)
                        bump("eok")
                    @blk.vector
                    def _(v, t=t, b_eok=b_eok, b_pvz=b_pvz):
                        v.wait_ge(S("eok"), b_eok + 1)
                        v.wait_ge(S("pvz"), b_pvz)
                        v.scalar_tensor_tensor(out=pflat[:, :], in0=sg[:, t, :],
                                               scalar=lo[:, t:t + 1],
                                               in1=et[:, :],
                                               op0=ALU.is_ge,
                                               op1=ALU.mult).then_inc(S("pok"), 1)
                        bump("pok")
                    @blk.tensor
                    def _(te, b_pok=b_pok, b_pte=b_pte):
                        te.wait_ge(S("pok"), b_pok + 1)
                        te.wait_ge(S("pte"), b_pte)
                        for half in range(2):
                            if half == 1:
                                te.wait_ge(S("pte"), b_pte + 1)
                            for b2 in range(16):
                                jc = half * 16 + b2
                                inst = te.transpose(ptp[:, b2 * 128:(b2 + 1) * 128],
                                                    ptr[:, jc * 128:(jc + 1) * 128],
                                                    eye16[:, :])
                                if b2 == 15:
                                    inst.then_inc(S("ptm"), 1)
                            bump("ptm")
                    @blk.vector
                    def _(v, b_ptm=b_ptm):
                        for half in range(2):
                            v.wait_ge(S("ptm"), b_ptm + half + 1)
                            v.tensor_copy(
                                ptr[:, half * 2048:(half + 1) * 2048],
                                ptp[:, :]).then_inc(S("pte"), 1)
                            bump("pte")
                    @blk.tensor
                    def _(te, h=h, t=t, b_pte=b_pte, b_aosc=b_aosc):
                        te.wait_ge(S("pte"), b_pte + 2)
                        te.wait_ge(S("aosc"), b_aosc)
                        te.wait_ge(S("val"), cur("val"))
                        for jc in range(JC):
                            inst = te.matmul(ppv[:, t * 128:(t + 1) * 128],
                                             va01[h % 2][:, jc * 65:(jc + 1) * 65],
                                             ptr[:, jc * 128:(jc + 1) * 128],
                                             start=(jc == 0), stop=(jc == JC - 1))
                            if jc == JC - 1:
                                inst.then_inc(S("pvz"), 1)
                        bump("pvz")

            if dbg is not None and dbg.startswith("et") and h == int(dbg[2:] or 0) and g == 0:
                return _dbg_dump([(et[:, i * 1024:(i + 1) * 1024], True)
                                  for i in range(4)])
            if dbg is not None and dbg.startswith("pt") and h == int(dbg[2:] or 0) and g == 0:
                return _dbg_dump([(pflat[:, i * 1024:(i + 1) * 1024], True)
                                  for i in range(4)])
            if dbg == "pv" and h == 0 and g == 0:
                return _dbg_dump([(ppv[0:65, :], False)])
            with nc.Block() as blk:
                @blk.vector
                def _(v, bz=cur("pvz")):
                    v.wait_ge(S("pvz"), bz)
                    v.tensor_copy(tx[0:65, 0:512], ppv[0:65, :])
            with nc.Block() as blk:
                @blk.vector
                def _(v):
                    v.tensor_scalar(zrow2[:, :], tx[0:65, 0:512], 1e-30, None,
                                    op0=ALU.max)
                    v.drain()
                    v.reciprocal(zrow[:, :], zrow2[:, :]).then_inc(S("zok"), 1)
                    bump("zok")
                @blk.tensor
                def _(te, b_zbc=cur("zbc")):
                    te.wait_ge(S("zok"), cur("zok"))
                    te.wait_ge(S("zbc"), b_zbc)
                    te.matmul(pm[0:64, :], ones65[64:65, 0:64], zrow[64:65, :],
                              start=True, stop=True).then_inc(S("zbm"), 1)
                    bump("zbm")
                @blk.vector
                def _(v, h=h, g=g, bd0=cur("aodma")):
                    v.wait_ge(S("zbm"), cur("zbm"))
                    v.wait_ge(S("aodma"), bd0)
                    v.tensor_copy(zb[:, :], pm[0:64, :]).then_inc(S("zbc"), 1)
                    bump("zbc")
                    v.drain()
                    for t in range(GRP):
                        inst = v.tensor_tensor(
                            out=aotmp[:, t * 128:(t + 1) * 128],
                            in0=ppv[0:64, t * 128:(t + 1) * 128],
                            in1=zb[:, t * 128:(t + 1) * 128],
                            op=ALU.mult)
                        if t == GRP - 1:
                            inst.then_inc(S("aosc"), 1)
                    bump("aosc")
                @blk.sync
                def _(sy, g=g, hp=hp, hc=hc, ba=cur("aosc"), bd=cur("aodma")):
                    sy.wait_ge(S("aosc"), ba)
                    sy.dma_start(out=aots[hc][hp:hp + 64, g * 512:(g + 1) * 512],
                                 in_=aotmp[:, :]).then_inc(S("aodma"), 16)
                    bump("aodma", 16)
            if dbg is not None and dbg.startswith("va") and h == int(dbg[2:] or 0) and g == 0:
                vaf = va01[h % 2]
                return _dbg_dump([(vaf[:, 0:1024], True),
                                  (vaf[:, 1024:2048], True),
                                  (vaf[:, 2048:2080], True)])
            if dbg is not None and dbg.startswith("zb") and h == int(dbg[2:] or 0) and g == 0:
                with nc.Block() as blk2:
                    @blk2.sync
                    def _(sy):
                        sy.dma_start(out=out[600:601, 0:512],
                                     in_=zrow2[64:65, :]).then_inc(S("st"), 16)
                        sy.dma_start(out=out[601:602, 0:512],
                                     in_=zrow[64:65, :]).then_inc(S("st"), 16)
                        bump("st", 32)
                        sy.wait_ge(S("st"), cur("st"))
                return _dbg_dump([(zb[:, :], False), (aotmp[:, :], True)])

    with nc.Block() as blk:
        @blk.tensor
        def _(te):
            te.wait_ge(S("aodma"), cur("aodma"))
    if dbg == "h3":
        return _dbg_dump([(aots[1][:, 0:1024], True), (aots[1][:, 1024:2048], True)])
    if dbg == "kchk":
        return _dbg_dump([(kts[1][:, 0:1024], True), (kts[1][:, 1024:2048], True),
                          (qts[1][:, 0:1024], True), (kts[0][:, 0:1024], True)])
    if dbg == "ao":
        return _dbg_dump([(aots[dc][:, 0:1024], True) for dc in range(DC)])
    free("neglo", "tt4", "tt3", "tt2", "tt1", "g01", "cnt", "mid", "hi", "lo",
         "zb", "aotmp", "ones65", "zrow2", "zrow", "va1", "va0", "ptr", "et", "sg",
         *["k16_%d" % i for i in reversed(range(DC))])
    if stop_phase <= 2:
        return _finish_stub(nc, out, ao16, S, bump, cur, live, sems)

    res = alloc("res", [128, DC, TOWN], F32)
    stat = alloc("stat", [1, 4, 512], F32)
    emit_proj(woutT, 0, 8,
              lambda dcc, c0: aots[dcc][:, c0:c0 + 512], 2,
              lambda ec, half: res[:, ec, half * 1024:(half + 1) * 1024],
              "o", evac="scalar", act=ACT.Copy, scale=1.0)
    with nc.Block() as blk:
        @blk.vector
        def _(v):
            for dc in range(DC):
                v.tensor_scalar(res[:, dc, :], res[:, dc, :],
                                biasb[:, dc:dc + 1], None, op0=ALU.add)

    def emit_residual(xsrc):
        for dc in range(DC):
            with nc.Block() as blk:
                @blk.sync
                def _(sy, dc=dc):
                    sy.dma_start(out=tx[:, 0:TOWN],
                                 in_=xsrc[dc * 128:(dc + 1) * 128, :]
                                 ).then_inc(S("lr"), 16)
                    bump("lr", 16)
                @blk.vector
                def _(v, dc=dc):
                    v.wait_ge(S("lr"), cur("lr"))
                    v.tensor_tensor(out=res[:, dc, :], in0=res[:, dc, :],
                                    in1=tx[:, 0:TOWN], op=ALU.add)

    def emit_ln(tag):
        for f in range(TOWN // 512):
            c0 = f * 512
            b1 = cur("lna"); b2 = cur("lnb"); b3 = cur("lnc")
            with nc.Block() as blk:
                @blk.tensor
                def _(te, c0=c0):
                    for dc in range(DC):
                        inst = te.matmul(pm[0:1, :], onesc[:, :],
                                         res[:, dc, c0:c0 + 512],
                                         start=(dc == 0), stop=(dc == DC - 1))
                        if dc == DC - 1:
                            inst.then_inc(S("lna"), 1)
                    bump("lna")
                @blk.vector
                def _(v, c0=c0, b1=b1, b4=cur("lnd0")):
                    for dc in range(4):
                        inst = v.tensor_tensor(out=tx[:, dc * 512:(dc + 1) * 512],
                                               in0=res[:, dc, c0:c0 + 512],
                                               in1=res[:, dc, c0:c0 + 512],
                                               op=ALU.mult)
                        if dc == 3:
                            inst.then_inc(S("lnb"), 1)
                    bump("lnb")
                    v.wait_ge(S("lna"), b1 + 1)
                    v.tensor_copy(stat[:, 0, :], pm[0:1, :]).then_inc(S("lnc"), 1)
                    bump("lnc")
                    v.wait_ge(S("lnd0"), b4 + 1)
                    for dc in range(4):
                        inst = v.tensor_tensor(out=tx[:, dc * 512:(dc + 1) * 512],
                                               in0=res[:, dc + 4, c0:c0 + 512],
                                               in1=res[:, dc + 4, c0:c0 + 512],
                                               op=ALU.mult)
                        if dc == 3:
                            inst.then_inc(S("lnb"), 1)
                    bump("lnb")
                @blk.tensor
                def _(te, b2=b2, b3=b3):
                    te.wait_ge(S("lnb"), b2 + 1)
                    te.wait_ge(S("lnc"), b3 + 1)
                    for dc in range(4):
                        inst = te.matmul(pm[0:1, :], onesc[:, :],
                                         tx[:, dc * 512:(dc + 1) * 512],
                                         start=(dc == 0), stop=False)
                        if dc == 3:
                            inst.then_inc(S("lnd0"), 1)
                    bump("lnd0")
                    te.wait_ge(S("lnb"), b2 + 2)
                    for dc in range(4):
                        inst = te.matmul(pm[0:1, :], onesc[:, :],
                                         tx[:, dc * 512:(dc + 1) * 512],
                                         start=False, stop=(dc == 3))
                        if dc == 3:
                            inst.then_inc(S("lnd"), 1)
                    bump("lnd")
                @blk.vector
                def _(v):
                    v.wait_ge(S("lnd"), cur("lnd"))
                    v.tensor_copy(stat[:, 1, :], pm[0:1, :])
                    v.drain()
                    v.tensor_scalar(stat[:, 2, :], stat[:, 0, :],
                                    1.0 / D, None, op0=ALU.mult)
                    v.tensor_scalar(stat[:, 1, :], stat[:, 1, :],
                                    1.0 / D, None, op0=ALU.mult)
                    v.drain()
                    v.tensor_tensor(out=stat[:, 0, :], in0=stat[:, 2, :],
                                    in1=stat[:, 2, :], op=ALU.mult)
                    v.drain()
                    v.tensor_tensor(out=stat[:, 1, :], in0=stat[:, 1, :],
                                    in1=stat[:, 0, :], op=ALU.subtract)
                    v.drain()
                    v.tensor_scalar(stat[:, 1, :], stat[:, 1, :],
                                    1e-5, None, op0=ALU.add).then_inc(S("lne"), 1)
                    bump("lne")
                @blk.scalar
                def _(sc):
                    sc.wait_ge(S("lne"), cur("lne"))
                    sc.activation(out=stat[:, 0, :], in_=stat[:, 1, :],
                                  func=ACT.Ln, scale=1.0)
                    sc.drain()
                    sc.activation(out=stat[:, 3, :], in_=stat[:, 0, :],
                                  func=ACT.Exp, scale=-0.5).then_inc(S("lnf"), 1)
                    bump("lnf")
                @blk.tensor
                def _(te):
                    te.wait_ge(S("lnf"), cur("lnf"))
                    te.matmul(pa[:, 0:512], ones1[:, :], stat[:, 2, :],
                              start=True, stop=True)
                    te.matmul(pm[:, :], ones1[:, :], stat[:, 3, :],
                              start=True, stop=True).then_inc(S("lng"), 1)
                    bump("lng")
                @blk.vector
                def _(v, c0=c0):
                    v.wait_ge(S("lng"), cur("lng"))
                    for dc in range(DC):
                        v.tensor_tensor(out=tx[:, 0:512],
                                        in0=res[:, dc, c0:c0 + 512],
                                        in1=pa[:, 0:512], op=ALU.subtract)
                        v.drain()
                        v.tensor_tensor(out=res[:, dc, c0:c0 + 512],
                                        in0=tx[:, 0:512],
                                        in1=pm[:, :], op=ALU.mult)
                        v.drain()

    emit_residual(xq)
    emit_ln("n1")

    xn16 = alloc("xn16", [128, DC, TOWN], BF16)
    h16 = alloc("h16", [128, DC, TOWN], BF16)
    tf = alloc("tf", [128, 1024], F32)
    with nc.Block() as blk:
        @blk.vector
        def _(v):
            for dc in range(DC):
                v.tensor_copy(xn16[:, dc, :], res[:, dc, :])

    emit_proj(w1d, 0, 8,
              lambda dcc, c0: xn16[:, dcc, c0:c0 + 512], 2,
              lambda ec, half: h16[:, ec, half * 1024:(half + 1) * 1024],
              "f1", evac="scalar", act=ACT.Copy, scale=1.0)
    emit_proj(w2d, 0, 8,
              lambda dcc, c0: h16[:, dcc, c0:c0 + 512], 2,
              lambda ec, half: xn16[:, ec, half * 1024:(half + 1) * 1024],
              "f2", evac="scalar", act=ACT.Copy, scale=1.0)
    bw3 = cur("wld"); bo3 = cur("wok")
    bm3 = cur("mm"); be3 = cur("ev"); ba3 = cur("fadd")
    with nc.Block() as blk:
        @blk.sync
        def _(sy):
            for i in range(8):
                if i >= 1:
                    sy.wait_ge(S("wok"), bo3 + i)
                for dcc in range(DC):
                    sy.dma_start(
                        out=wf[:, dcc * 128:(dcc + 1) * 128],
                        in_=w3d[dcc * 128:(dcc + 1) * 128,
                                i * 128:(i + 1) * 128],
                    ).then_inc(S("wld"), 16)
                bump("wld", 16 * DC)
        def emit_cast(v, i):
            v.wait_ge(S("wld"), bw3 + 16 * DC * (i + 1))
            if i >= 2:
                v.wait_ge(S("mm"), bm3 + (i - 1) * 2)
            v.tensor_copy(w16[i % 2][:, :], wf[:, :]).then_inc(S("wok"), 1)
            bump("wok")

        def emit_add(v, m):
            i, half = m // 2, m % 2
            v.wait_ge(S("ev"), be3 + m + 1)
            v.tensor_tensor(out=res[:, i, half * 1024:(half + 1) * 1024],
                            in0=res[:, i, half * 1024:(half + 1) * 1024],
                            in1=tf[:, :], op=ALU.add).then_inc(S("fadd"), 1)
            bump("fadd")

        @blk.vector
        def _(v):
            emit_cast(v, 0)
            emit_cast(v, 1)
            for i in range(2, 8):
                emit_add(v, 2 * i - 4)
                emit_add(v, 2 * i - 3)
                emit_cast(v, i)
            for m in range(12, 16):
                emit_add(v, m)
        @blk.tensor
        def _(te):
            m = 0
            for i in range(8):
                te.wait_ge(S("wok"), bo3 + i + 1)
                for half in range(2):
                    psu = pa if m % 2 == 0 else pb
                    if m >= 2:
                        te.wait_ge(S("ev"), be3 + m - 1)
                    for dcc in range(DC):
                        for f in range(2):
                            c0 = half * 1024 + f * 512
                            inst = te.matmul(psu[:, f * 512:(f + 1) * 512],
                                             w16[i % 2][:, dcc * 128:(dcc + 1) * 128],
                                             xn16[:, dcc, c0:c0 + 512],
                                             start=(dcc == 0), stop=(dcc == DC - 1))
                            if dcc == DC - 1 and f == 1:
                                inst.then_inc(S("mm"), 1)
                    bump("mm")
                    m += 1
        @blk.scalar
        def _(sc):
            m = 0
            for i in range(8):
                for half in range(2):
                    psu = pa if m % 2 == 0 else pb
                    sc.wait_ge(S("mm"), bm3 + m + 1)
                    if m >= 1:
                        sc.wait_ge(S("fadd"), ba3 + m)
                    sc.activation(out=tf[:, :], in_=psu[:, :], func=ACT.Tanh,
                                  scale=1.0,
                                  bias=fbb[:, i:i + 1]).then_inc(S("ev"), 1)
                    bump("ev")
                    m += 1
    free("tf", "h16", "xn16")
    emit_ln("n2")

    for tt in range(QT):
        b_tr = cur("otr"); b_te = cur("ote")
        with nc.Block() as blk:
            @blk.tensor
            def _(te, tt=tt):
                for dc in range(DC):
                    inst = te.transpose(pa[:, dc * 128:(dc + 1) * 128],
                                        res[:, dc, tt * 128:(tt + 1) * 128],
                                        eye32[:, :])
                    if dc == DC - 1:
                        inst.then_inc(S("otr"), 1)
                bump("otr")
            @blk.vector
            def _(v, b_tr=b_tr, tt=tt):
                v.wait_ge(S("otr"), b_tr + 1)
                v.wait_ge(S("st"), 16 * tt)
                v.tensor_copy(tx[:, 0:1024], pa[:, :]).then_inc(S("ote"), 1)
                bump("ote")
            @blk.sync
            def _(sy, tt=tt, b_te=b_te):
                sy.wait_ge(S("ote"), b_te + 1)
                sy.dma_start(out=out[tt * 128:(tt + 1) * 128, :],
                             in_=tx[:, 0:1024]).then_inc(S("st"), 16)
                bump("st", 16)

    with nc.Block() as blk:
        @blk.sync
        def _(sy):
            sy.wait_ge(S("st"), cur("st"))

    for n in reversed(list(live)):
        live.pop(n).__exit__(None, None, None)
    for name in list(sems):
        sems.pop(name)[0].__exit__(None, None, None)
    return nc


def _finish_stub(nc, out, src16, S, bump, cur, live, sems):
    with (nc.sbuf_tensor("stub", [128, 1024], F32) as stub,
          nc.Block() as blk):
        @blk.vector
        def _(v):
            for dc in range(DC):
                v.wait_ge(S("st"), 16 * dc)
                v.tensor_copy(stub[:, :], src16[:, dc, 0:1024]).then_inc(S("stub"), 1)
                bump("stub")
        @blk.sync
        def _(sy):
            for dc in range(DC):
                sy.wait_ge(S("stub"), dc + 1)
                sy.dma_start(out=out[dc * 128:(dc + 1) * 128, :],
                             in_=stub[:, :]).then_inc(S("st"), 16)
                bump("st", 16)
            sy.wait_ge(S("st"), cur("st"))
    for n in reversed(list(live)):
        live.pop(n).__exit__(None, None, None)
    for name in list(sems):
        sems.pop(name)[0].__exit__(None, None, None)
    return nc


_NC_CACHE = None

def kernel(L0_attn_x, L0_attn_W_qkv, L0_attn_W_out, L0_attn_b_out,
           L0_ffn_weight, L0_ffn_aux_weight1, L0_ffn_aux_weight2, L0_ffn_bias):
    global _NC_CACHE
    x = np.asarray(L0_attn_x, dtype=np.float32)
    wqkv = np.asarray(L0_attn_W_qkv, dtype=np.float32)
    wout = np.asarray(L0_attn_W_out, dtype=np.float32)
    b_out = np.asarray(L0_attn_b_out, dtype=np.float32)
    w1 = np.asarray(L0_ffn_weight, dtype=np.float32)
    w2 = np.asarray(L0_ffn_aux_weight1, dtype=np.float32)
    w3 = np.asarray(L0_ffn_aux_weight2, dtype=np.float32)
    fb = np.asarray(L0_ffn_bias, dtype=np.float32)

    x_fm = np.ascontiguousarray(np.transpose(x, (0, 2, 1)))
    wqkvT = np.ascontiguousarray(wqkv.T)
    woutT = np.ascontiguousarray(wout.T)
    eye = np.eye(128, dtype=np.float32)
    boutr = np.ascontiguousarray(b_out.reshape(DC, 128))
    fbr = np.ascontiguousarray(fb.reshape(DC, 128))

    if _NC_CACHE is None:
        _NC_CACHE = build_nc()
    nc = _NC_CACHE

    in_maps = []
    for c in range(8):
        b, half = c // 2, c % 2
        in_maps.append({
            "xq": np.ascontiguousarray(x_fm[b][:, half * TOWN:(half + 1) * TOWN]),
            "xfull": x_fm[b],
            "wqkvT": wqkvT,
            "woutT": woutT,
            "boutd": boutr,
            "w1d": w1, "w2d": w2, "w3d": w3,
            "fbiasd": fbr,
            "eyed": eye,
        })
    res = run_bass_kernel_spmd(nc, in_maps, core_ids=list(range(8)))
    outp = np.empty((B, T, D), dtype=np.float32)
    for c in range(8):
        b, half = c // 2, c % 2
        outp[b, half * TOWN:(half + 1) * TOWN, :] = res.results[c]["out"]
    return outp


# version 07
# speedup vs baseline: 1.6848x; 1.6848x over previous
import numpy as np
import concourse.bass as bass
import concourse.mybir as mybir
from concourse.bass_utils import run_bass_kernel_spmd

F32 = mybir.dt.float32
BF16 = mybir.dt.bfloat16
ALU = mybir.AluOpType
ACT = mybir.ActivationFunctionType

B, T, D, H = 4, 4096, 1024, 16
TOWN = 2048
NBIS = 38
GRP = 4
DC = D // 128
QT = TOWN // 128
JC = T // 128
CNT_THR = 408.5
SGN_THR = 2.0 * CNT_THR - 4096.0


def build_nc(stop_phase=99, dbg=None, hset=None, hrun=None):
    nc = bass.Bass("TRN2", target_bir_lowering=False,
                   detect_race_conditions=True,
                   dynamic_dma_scratch_size=2048)
    xq = nc.dram_tensor("xq", [D, TOWN], F32, kind="ExternalInput").ap()
    xfull = nc.dram_tensor("xfull", [D, T], F32, kind="ExternalInput").ap()
    wqkvT = nc.dram_tensor("wqkvT", [D, 3 * D], F32, kind="ExternalInput").ap()
    woutT = nc.dram_tensor("woutT", [D, D], F32, kind="ExternalInput").ap()
    boutd = nc.dram_tensor("boutd", [DC, 128], F32, kind="ExternalInput").ap()
    w1d = nc.dram_tensor("w1d", [D, D], F32, kind="ExternalInput").ap()
    w2d = nc.dram_tensor("w2d", [D, D], F32, kind="ExternalInput").ap()
    w3d = nc.dram_tensor("w3d", [D, D], F32, kind="ExternalInput").ap()
    fbiasd = nc.dram_tensor("fbiasd", [DC, 128], F32, kind="ExternalInput").ap()
    eyed = nc.dram_tensor("eyed", [128, 128], F32, kind="ExternalInput").ap()
    vaug = nc.dram_tensor("vaug", [H, 128, JC, 65], BF16, kind="Internal").ap()
    out = nc.dram_tensor("out", [TOWN, D], F32, kind="ExternalOutput").ap()

    sems = {}
    def S(name):
        if name not in sems:
            ctx = nc.semaphore(name)
            h = ctx.__enter__()
            sems[name] = [ctx, 0, h]
        return sems[name][2]
    def bump(name, k=1):
        S(name)
        sems[name][1] += k
        return sems[name][1]
    def cur(name):
        S(name)
        return sems[name][1]

    live = {}
    def alloc(name, shape, dt):
        t = nc.sbuf_tensor(name, shape, dt)
        h = t.__enter__()
        live[name] = t
        return h
    def free(*names):
        for n in names:
            live.pop(n).__exit__(None, None, None)

    def palloc(name, shape, dt):
        t = nc.psum_tensor(name, shape, dt)
        h = t.__enter__()
        return h

    pa = palloc("pa", [128, 1024], F32)
    pb = palloc("pb", [128, 1024], F32)
    ptp = palloc("ptp", [128, 2048], BF16)
    ppv = palloc("ppv", [65, 512], F32)
    pm = palloc("pm", [128, 512], F32)

    eye32 = alloc("eye32", [128, 128], F32)
    eye16 = alloc("eye16", [128, 128], BF16)
    ones1 = alloc("ones1", [1, 128], F32)
    onesc = alloc("onesc", [128, 1], F32)
    tx = alloc("tx", [128, TOWN], F32)
    wf = alloc("wf", [128, 1024], F32)
    w16 = [alloc("w16a", [128, 1024], BF16), alloc("w16b", [128, 1024], BF16)]
    biasb = alloc("biasb", [128, DC], F32)
    fbb = alloc("fbb", [128, DC], F32)

    qts = [alloc("q16_%d" % i, [128, TOWN], BF16) for i in range(DC)]
    aots = [alloc("ao16_%d" % i, [128, TOWN], BF16) for i in range(DC)]

    def emit_load_bias(sy, dst, src):
        for dc in range(DC):
            sy.dma_start(out=dst[:, dc:dc + 1],
                         in_=src[dc, :]).then_inc(S("ld"), 16)
            bump("ld", 16)

    def emit_proj(wsrc, wcol0, n_ec, mov, n_half, dst_write, tag,
                  evac="scalar", act=ACT.Copy, scale=1.0, bias_col=None):
        bw = cur("wld"); bo = cur("wok")
        bm = cur("mm"); be = cur("ev")
        with nc.Block() as blk:
            @blk.sync
            def _(sy):
                for i in range(n_ec):
                    if i >= 1:
                        sy.wait_ge(S("wok"), bo + i)
                    for dcc in range(DC):
                        sy.dma_start(
                            out=wf[:, dcc * 128:(dcc + 1) * 128],
                            in_=wsrc[dcc * 128:(dcc + 1) * 128,
                                     wcol0 + i * 128:wcol0 + (i + 1) * 128],
                        ).then_inc(S("wld"), 16)
                    bump("wld", 16 * DC)

            @blk.vector
            def _(v):
                for i in range(n_ec):
                    v.wait_ge(S("wld"), bw + 16 * DC * (i + 1))
                    if i >= 2:
                        v.wait_ge(S("mm"), bm + (i - 1) * n_half)
                    v.tensor_copy(w16[i % 2][:, :], wf[:, :]).then_inc(S("wok"), 1)
                    bump("wok")

            @blk.tensor
            def _(te):
                m = 0
                for i in range(n_ec):
                    te.wait_ge(S("wok"), bo + i + 1)
                    for half in range(n_half):
                        psu = pa if m % 2 == 0 else pb
                        if m >= 2:
                            te.wait_ge(S("ev"), be + m - 1)
                        for dcc in range(DC):
                            for f in range(2):
                                c0 = half * 1024 + f * 512
                                inst = te.matmul(psu[:, f * 512:(f + 1) * 512],
                                                 w16[i % 2][:, dcc * 128:(dcc + 1) * 128],
                                                 mov(dcc, c0),
                                                 start=(dcc == 0), stop=(dcc == DC - 1))
                                if dcc == DC - 1 and f == 1:
                                    inst.then_inc(S("mm"), 1)
                        bump("mm")
                        m += 1

            @blk.scalar
            def _(sc):
                m = 0
                for i in range(n_ec):
                    for half in range(n_half):
                        psu = pa if m % 2 == 0 else pb
                        sc.wait_ge(S("mm"), bm + m + 1)
                        if bias_col is None:
                            sc.activation(out=dst_write(i, half),
                                          in_=psu[:, :], func=act,
                                          scale=scale).then_inc(S("ev"), 1)
                        else:
                            sc.activation(out=dst_write(i, half),
                                          in_=psu[:, :], func=act,
                                          scale=scale,
                                          bias=bias_col(i)).then_inc(S("ev"), 1)
                        bump("ev")
                        m += 1

    xq16 = alloc("xq16", [128, DC, TOWN], BF16)
    with nc.Block() as blk:
        @blk.sync
        def _(sy):
            sy.dma_start(out=eye32[:, :], in_=eyed[:, :]).then_inc(S("ld"), 16)
            bump("ld", 16)
            emit_load_bias(sy, biasb, boutd)
            emit_load_bias(sy, fbb, fbiasd)
        @blk.vector
        def _(v):
            v.wait_ge(S("ld"), cur("ld"))
            v.tensor_copy(eye16[:, :], eye32[:, :])
            v.memset(ones1[:, :], 1.0)
            v.memset(onesc[:, :], 1.0)

    for dc in range(DC):
        with nc.Block() as blk:
            @blk.sync
            def _(sy, dc=dc):
                sy.dma_start(out=tx[:, 0:TOWN],
                             in_=xq[dc * 128:(dc + 1) * 128, :]).then_inc(S("lx"), 16)
                bump("lx", 16)
            @blk.vector
            def _(v, dc=dc):
                v.wait_ge(S("lx"), cur("lx"))
                v.tensor_copy(xq16[:, dc, :], tx[:, 0:TOWN])

    emit_proj(wqkvT, 0, 8,
              lambda dcc, c0: xq16[:, dcc, c0:c0 + 512], 2,
              lambda ec, half: qts[ec][:, half * 1024:(half + 1) * 1024],
              "q", evac="scalar", act=ACT.Copy, scale=0.03125)
    free("xq16")
    if stop_phase <= 0:
        return _finish_stub(nc, out, q16, S, bump, cur, live, sems)

    kts = [alloc("k16_%d" % i, [128, T], BF16) for i in range(DC)]
    x16 = alloc("x16", [128, DC, T], BF16)
    for dc in range(DC):
        for hh in range(2):
            with nc.Block() as blk:
                @blk.sync
                def _(sy, dc=dc, hh=hh):
                    sy.dma_start(out=tx[:, :],
                                 in_=xfull[dc * 128:(dc + 1) * 128,
                                           hh * TOWN:(hh + 1) * TOWN]
                                 ).then_inc(S("lx2"), 16)
                    bump("lx2", 16)
                @blk.vector
                def _(v, dc=dc, hh=hh):
                    v.wait_ge(S("lx2"), cur("lx2"))
                    v.tensor_copy(x16[:, dc, hh * TOWN:(hh + 1) * TOWN], tx[:, :])

    emit_proj(wqkvT, D, 8,
              lambda dcc, c0: x16[:, dcc, c0:c0 + 512], 4,
              lambda ec, half: kts[ec][:, half * 1024:(half + 1) * 1024],
              "k", evac="scalar", act=ACT.Copy, scale=1.0)

    v16 = alloc("v16", [128, T], BF16)
    vtt = alloc("vtt", [128, JC, 65], BF16)
    for ec in range(8):
        emit_proj(wqkvT, 2 * D + ec * 128 - 0, 1,
                  lambda dcc, c0: x16[:, dcc, c0:c0 + 512], 4,
                  lambda _i, half: v16[:, half * 1024:(half + 1) * 1024],
                  "v%d" % ec, evac="scalar", act=ACT.Relu, scale=1.0)
        for hp in range(2):
            h = ec * 2 + hp
            with nc.Block() as blk:
                @blk.tensor
                def _(te, hp=hp):
                    for blkj in range(JC):
                        inst = te.transpose(ptp[:, blkj * 64:(blkj + 1) * 64],
                                            v16[hp * 64:(hp + 1) * 64,
                                                blkj * 128:(blkj + 1) * 128],
                                            eye16[hp * 64:(hp + 1) * 64,
                                                  hp * 64:(hp + 1) * 64])
                        if blkj == JC - 1:
                            inst.then_inc(S("vtr"), 1)
                    bump("vtr")
                @blk.vector
                def _(v, h=h):
                    v.wait_ge(S("vtr"), cur("vtr"))
                    v.wait_ge(S("vsp"), 16 * h)
                    v.tensor_copy(vtt[:, :, 0:64],
                                  ptp[:, :].rearrange("p (a b) -> p a b", b=64))
                    v.memset(vtt[:, :, 64:65], 1.0).then_inc(S("vttok"), 1)
                    bump("vttok")
                @blk.sync
                def _(sy, h=h):
                    sy.wait_ge(S("vttok"), cur("vttok"))
                    sy.dma_start(out=vaug[h], in_=vtt[:, :, :]).then_inc(S("vsp"), 16)
                    bump("vsp", 16)
    free("vtt", "v16", "x16")
    if stop_phase <= 1:
        return _finish_stub(nc, out, k16, S, bump, cur, live, sems)

    sg = alloc("sg", [128, GRP, T], BF16)
    et = alloc("et", [128, T], BF16)
    ptr = alloc("ptr", [128, JC * 128], BF16)
    pflat = ptr
    va01 = [alloc("va0", [128, JC * 65], BF16), alloc("va1", [128, JC * 65], BF16)]
    zrow = alloc("zrow", [65, 512], F32)
    zrow2 = alloc("zrow2", [65, 512], F32)
    ones65 = alloc("ones65", [65, 128], F32)
    aotmp = alloc("aotmp", [64, 512], BF16)
    zb = alloc("zb", [64, 512], F32)
    lo = alloc("lo", [128, GRP], F32)
    hi = alloc("hi", [128, GRP], F32)
    mid = alloc("mid", [128, GRP], F32)
    cnt = alloc("cnt", [128, GRP], F32)
    g01 = alloc("g01", [128, GRP], F32)
    tt1 = alloc("tt1", [128, GRP], F32)
    tt2 = alloc("tt2", [128, GRP], F32)
    tt3 = alloc("tt3", [128, GRP], F32)
    tt4 = alloc("tt4", [128, GRP], F32)
    neglo = alloc("neglo", [128, GRP], F32)
    nmid = alloc("nmid", [128, GRP], F32)
    scnt4 = alloc("scnt4", [128, 2], F32)
    ssc0 = alloc("ssc0", [128, T], BF16)


    def _dbg_dump(aps):
        with nc.Block() as blk:
            @blk.vector
            def _(v):
                for i, (ap, _) in enumerate(aps):
                    v.wait_ge(S("st"), 16 * i)
                    v.tensor_copy(tx[0:ap.shape[0], 0:ap.shape[-1]], ap).then_inc(S("stub"), 1)
                    bump("stub")
            @blk.sync
            def _(sy):
                for i, (ap, _) in enumerate(aps):
                    sy.wait_ge(S("stub"), i + 1)
                    sy.dma_start(out=out[i * 128:i * 128 + ap.shape[0], 0:ap.shape[-1]],
                                 in_=tx[0:ap.shape[0], 0:ap.shape[-1]]).then_inc(S("st"), 16)
                    bump("st", 16)
                sy.wait_ge(S("st"), cur("st"))
        for n in reversed(list(live)):
            live.pop(n).__exit__(None, None, None)
        for name in list(sems):
            sems.pop(name)[0].__exit__(None, None, None)
        return nc

    with nc.Block() as blk:
        @blk.sync
        def _(sy):
            sy.wait_ge(S("vsp"), cur("vsp"))
            for hh in (hset[:2] if hset else [0]):
                sy.dma_start(out=va01[hh % 2].rearrange("p (a b) -> p a b", b=65),
                             in_=vaug[hh]).then_inc(S("val"), 16)
                bump("val", 16)
        @blk.vector
        def _(v):
            v.memset(ones65[:, :], 1.0)

    with nc.Block() as blk:
        @blk.tensor
        def _(te):
            te.wait_ge(S("ev"), cur("ev"))

    HRUN = hrun if hrun else (2 if dbg == "kchk" else (3 if dbg == "h3" else H))
    for h in (hset if hset else range(HRUN)):
        hc, hp = h // 2, (h % 2) * 64
        if hset is None and h + 1 < H:
            with nc.Block() as blk:
                @blk.sync
                def _(sy, h=h, b_pvz=cur("pvz")):
                    sy.wait_ge(S("pvz"), b_pvz)
                    sy.dma_start(out=va01[(h + 1) % 2].rearrange("p (a b) -> p a b", b=65),
                                 in_=vaug[h + 1]).then_inc(S("val"), 16)
                    bump("val", 16)
        for g in range(QT // GRP):
            base_mm = cur("smm")
            base_ev = cur("sev")
            with nc.Block() as blk:
                @blk.tensor
                def _(te, h=h, g=g, hc=hc, hp=hp, base_ev=base_ev):
                    te.wait_ge(S("sev"), base_ev)
                    m = 0
                    for t in range(GRP):
                        qt = g * GRP + t
                        for quad in range(4):
                            psu = pa if m % 2 == 0 else pb
                            if m >= 2:
                                te.wait_ge(S("sev"), base_ev + m - 1)
                            for f in range(2):
                                j0 = quad * 1024 + f * 512
                                inst = te.matmul(psu[:, f * 512:(f + 1) * 512],
                                                 qts[hc][hp:hp + 64,
                                                         qt * 128:(qt + 1) * 128],
                                                 kts[hc][hp:hp + 64, j0:j0 + 512],
                                                 start=True, stop=True)
                                if f == 1:
                                    inst.then_inc(S("smm"), 1)
                            bump("smm")
                            m += 1
                @blk.scalar
                def _(sc, base_mm=base_mm, b_pokg=cur("pok")):
                    sc.wait_ge(S("pok"), b_pokg)
                    m = 0
                    for t in range(GRP):
                        for quad in range(4):
                            psu = pa if m % 2 == 0 else pb
                            sc.wait_ge(S("smm"), base_mm + m + 1)
                            sc.activation(out=sg[:, t, quad * 1024:(quad + 1) * 1024],
                                          in_=psu[:, :], func=ACT.Copy,
                                          scale=1.0).then_inc(S("sev"), 1)
                            bump("sev")
                            m += 1

            if dbg is not None and dbg.startswith("sg") and h == int(dbg[2:] or 0) and g == 0:
                return _dbg_dump([(sg[:, 0, i * 1024:(i + 1) * 1024], True)
                                  for i in range(4)])
            b_scnt = cur("scnt")
            b_bmid = cur("bmid")
            with nc.Block() as blk:
                @blk.vector
                def _(v, b_sev=cur("sev"), b_eokg=cur("eok"), b_pvzg=cur("pvz"),
                      b_scnt=b_scnt, b_bmid=b_bmid):
                    v.wait_ge(S("sev"), b_sev)
                    v.wait_ge(S("eok"), b_eokg)
                    v.wait_ge(S("pvz"), b_pvzg)
                    v.memset(mid[:, :], 4.0)
                    v.memset(nmid[:, :], -4.0).then_inc(S("bmid"), 1)
                    bump("bmid")
                    v.drain()
                    for it in range(NBIS):
                        sk = 8.0 * 2.0 ** (-(it + 2))
                        v.tensor_scalar(et[:, :], sg[:, 0, :],
                                        mid[:, 0:1], 0.0,
                                        op0=ALU.is_ge, op1=ALU.add,
                                        accum_out=cnt[:, 0:1])
                        v.tensor_scalar(pflat[:, :], sg[:, 1, :],
                                        mid[:, 1:2], 0.0,
                                        op0=ALU.is_ge, op1=ALU.add,
                                        accum_out=cnt[:, 1:2])
                        v.drain()
                        v.wait_ge(S("scnt"), b_scnt + it + 1)
                        v.tensor_scalar(g01[:, 0:2], cnt[:, 0:2], CNT_THR, None,
                                        op0=ALU.is_gt)
                        v.tensor_scalar(g01[:, 2:4], scnt4[:, :], SGN_THR, None,
                                        op0=ALU.is_gt)
                        v.drain()
                        v.tensor_scalar(tt1[:, :], g01[:, :], 2.0 * sk, -sk,
                                        op0=ALU.mult, op1=ALU.add)
                        v.tensor_scalar(tt2[:, :], g01[:, :], -2.0 * sk, sk,
                                        op0=ALU.mult, op1=ALU.add)
                        v.drain()
                        v.tensor_tensor(out=mid[:, :], in0=mid[:, :],
                                        in1=tt1[:, :], op=ALU.add)
                        v.tensor_tensor(out=nmid[:, :], in0=nmid[:, :],
                                        in1=tt2[:, :], op=ALU.add
                                        ).then_inc(S("bmid"), 1)
                        bump("bmid")
                        v.drain()
                    cfin = 8.0 * 2.0 ** (-(NBIS + 1))
                    v.tensor_scalar(lo[:, :], mid[:, :], 1.0, -cfin,
                                    op0=ALU.mult, op1=ALU.add)
                    v.tensor_scalar(neglo[:, :], nmid[:, :], 1.0, cfin,
                                    op0=ALU.mult, op1=ALU.add).then_inc(S("bis"), 1)
                    bump("bis")
                    v.drain()
                @blk.scalar
                def _(sc, b_scnt=b_scnt, b_bmid=b_bmid):
                    for it in range(NBIS):
                        sc.wait_ge(S("bmid"), b_bmid + it + 1)
                        sc.activation(out=ssc0[:, :], in_=sg[:, 2, :],
                                      func=ACT.Sign, bias=nmid[:, 2:3],
                                      scale=1.0, accum_out=scnt4[:, 0:1])
                        sc.activation(out=ssc0[:, :], in_=sg[:, 3, :],
                                      func=ACT.Sign, bias=nmid[:, 3:4],
                                      scale=1.0, accum_out=scnt4[:, 1:2])
                        sc.copy(out=tt3[0:1, 0:1],
                                in_=scnt4[0:1, 0:1]).then_inc(S("scnt"), 1)
                        bump("scnt")

            if dbg is not None and dbg.startswith("bis") and h == int(dbg[3:] or 0) and g == 0:
                return _dbg_dump([(lo[:, :], False), (hi[:, :], False),
                                  (cnt[:, :], False), (neglo[:, :], False)])
            for t in range(GRP):
                qt = g * GRP + t
                b_eok = cur("eok"); b_pok = cur("pok")
                b_ptm = cur("ptm"); b_pte = cur("pte")
                b_pvz = cur("pvz"); b_bis = cur("bis"); b_aosc = cur("aosc")
                with nc.Block() as blk:
                    @blk.scalar
                    def _(sc, t=t, b_pok=b_pok, b_bis=b_bis):
                        sc.wait_ge(S("bis"), b_bis)
                        sc.wait_ge(S("pok"), b_pok)
                        sc.activation(out=et[:, :], in_=sg[:, t, :],
                                      func=ACT.Exp, bias=neglo[:, t:t + 1],
                                      scale=1.0).then_inc(S("eok"), 1)
                        bump("eok")
                    @blk.vector
                    def _(v, t=t, b_eok=b_eok, b_pvz=b_pvz):
                        v.wait_ge(S("eok"), b_eok + 1)
                        v.wait_ge(S("pvz"), b_pvz)
                        v.scalar_tensor_tensor(out=pflat[:, :], in0=sg[:, t, :],
                                               scalar=lo[:, t:t + 1],
                                               in1=et[:, :],
                                               op0=ALU.is_ge,
                                               op1=ALU.mult).then_inc(S("pok"), 1)
                        bump("pok")
                    @blk.tensor
                    def _(te, b_pok=b_pok, b_pte=b_pte):
                        te.wait_ge(S("pok"), b_pok + 1)
                        te.wait_ge(S("pte"), b_pte)
                        for half in range(2):
                            if half == 1:
                                te.wait_ge(S("pte"), b_pte + 1)
                            for b2 in range(16):
                                jc = half * 16 + b2
                                inst = te.transpose(ptp[:, b2 * 128:(b2 + 1) * 128],
                                                    ptr[:, jc * 128:(jc + 1) * 128],
                                                    eye16[:, :])
                                if b2 == 15:
                                    inst.then_inc(S("ptm"), 1)
                            bump("ptm")
                    @blk.vector
                    def _(v, b_ptm=b_ptm):
                        for half in range(2):
                            v.wait_ge(S("ptm"), b_ptm + half + 1)
                            v.tensor_copy(
                                ptr[:, half * 2048:(half + 1) * 2048],
                                ptp[:, :]).then_inc(S("pte"), 1)
                            bump("pte")
                    @blk.tensor
                    def _(te, h=h, t=t, b_pte=b_pte, b_aosc=b_aosc):
                        te.wait_ge(S("pte"), b_pte + 2)
                        te.wait_ge(S("aosc"), b_aosc)
                        te.wait_ge(S("val"), cur("val"))
                        for jc in range(JC):
                            inst = te.matmul(ppv[:, t * 128:(t + 1) * 128],
                                             va01[h % 2][:, jc * 65:(jc + 1) * 65],
                                             ptr[:, jc * 128:(jc + 1) * 128],
                                             start=(jc == 0), stop=(jc == JC - 1))
                            if jc == JC - 1:
                                inst.then_inc(S("pvz"), 1)
                        bump("pvz")

            if dbg is not None and dbg.startswith("et") and h == int(dbg[2:] or 0) and g == 0:
                return _dbg_dump([(et[:, i * 1024:(i + 1) * 1024], True)
                                  for i in range(4)])
            if dbg is not None and dbg.startswith("pt") and h == int(dbg[2:] or 0) and g == 0:
                return _dbg_dump([(pflat[:, i * 1024:(i + 1) * 1024], True)
                                  for i in range(4)])
            if dbg == "pv" and h == 0 and g == 0:
                return _dbg_dump([(ppv[0:65, :], False)])
            with nc.Block() as blk:
                @blk.vector
                def _(v, bz=cur("pvz")):
                    v.wait_ge(S("pvz"), bz)
                    v.tensor_copy(tx[0:65, 0:512], ppv[0:65, :])
            with nc.Block() as blk:
                @blk.vector
                def _(v):
                    v.tensor_scalar(zrow2[:, :], tx[0:65, 0:512], 1e-30, None,
                                    op0=ALU.max)
                    v.drain()
                    v.reciprocal(zrow[:, :], zrow2[:, :]).then_inc(S("zok"), 1)
                    bump("zok")
                @blk.tensor
                def _(te, b_zbc=cur("zbc")):
                    te.wait_ge(S("zok"), cur("zok"))
                    te.wait_ge(S("zbc"), b_zbc)
                    te.matmul(pm[0:64, :], ones65[64:65, 0:64], zrow[64:65, :],
                              start=True, stop=True).then_inc(S("zbm"), 1)
                    bump("zbm")
                @blk.vector
                def _(v, h=h, g=g, bd0=cur("aodma")):
                    v.wait_ge(S("zbm"), cur("zbm"))
                    v.wait_ge(S("aodma"), bd0)
                    v.tensor_copy(zb[:, :], pm[0:64, :]).then_inc(S("zbc"), 1)
                    bump("zbc")
                    v.drain()
                    for t in range(GRP):
                        inst = v.tensor_tensor(
                            out=aotmp[:, t * 128:(t + 1) * 128],
                            in0=ppv[0:64, t * 128:(t + 1) * 128],
                            in1=zb[:, t * 128:(t + 1) * 128],
                            op=ALU.mult)
                        if t == GRP - 1:
                            inst.then_inc(S("aosc"), 1)
                    bump("aosc")
                @blk.sync
                def _(sy, g=g, hp=hp, hc=hc, ba=cur("aosc"), bd=cur("aodma")):
                    sy.wait_ge(S("aosc"), ba)
                    sy.dma_start(out=aots[hc][hp:hp + 64, g * 512:(g + 1) * 512],
                                 in_=aotmp[:, :]).then_inc(S("aodma"), 16)
                    bump("aodma", 16)
            if dbg is not None and dbg.startswith("va") and h == int(dbg[2:] or 0) and g == 0:
                vaf = va01[h % 2]
                return _dbg_dump([(vaf[:, 0:1024], True),
                                  (vaf[:, 1024:2048], True),
                                  (vaf[:, 2048:2080], True)])
            if dbg is not None and dbg.startswith("zb") and h == int(dbg[2:] or 0) and g == 0:
                with nc.Block() as blk2:
                    @blk2.sync
                    def _(sy):
                        sy.dma_start(out=out[600:601, 0:512],
                                     in_=zrow2[64:65, :]).then_inc(S("st"), 16)
                        sy.dma_start(out=out[601:602, 0:512],
                                     in_=zrow[64:65, :]).then_inc(S("st"), 16)
                        bump("st", 32)
                        sy.wait_ge(S("st"), cur("st"))
                return _dbg_dump([(zb[:, :], False), (aotmp[:, :], True)])

    with nc.Block() as blk:
        @blk.tensor
        def _(te):
            te.wait_ge(S("aodma"), cur("aodma"))
    if dbg == "h3":
        return _dbg_dump([(aots[1][:, 0:1024], True), (aots[1][:, 1024:2048], True)])
    if dbg == "kchk":
        return _dbg_dump([(kts[1][:, 0:1024], True), (kts[1][:, 1024:2048], True),
                          (qts[1][:, 0:1024], True), (kts[0][:, 0:1024], True)])
    if dbg == "ao":
        return _dbg_dump([(aots[dc][:, 0:1024], True) for dc in range(DC)])
    free("ssc0", "scnt4", "nmid",
         "neglo", "tt4", "tt3", "tt2", "tt1", "g01", "cnt", "mid", "hi", "lo",
         "zb", "aotmp", "ones65", "zrow2", "zrow", "va1", "va0", "ptr", "et", "sg",
         *["k16_%d" % i for i in reversed(range(DC))])
    if stop_phase <= 2:
        return _finish_stub(nc, out, ao16, S, bump, cur, live, sems)

    res = alloc("res", [128, DC, TOWN], F32)
    stat = alloc("stat", [1, 4, 512], F32)
    emit_proj(woutT, 0, 8,
              lambda dcc, c0: aots[dcc][:, c0:c0 + 512], 2,
              lambda ec, half: res[:, ec, half * 1024:(half + 1) * 1024],
              "o", evac="scalar", act=ACT.Copy, scale=1.0)
    with nc.Block() as blk:
        @blk.vector
        def _(v):
            for dc in range(DC):
                v.tensor_scalar(res[:, dc, :], res[:, dc, :],
                                biasb[:, dc:dc + 1], None, op0=ALU.add)

    def emit_residual(xsrc):
        for dc in range(DC):
            with nc.Block() as blk:
                @blk.sync
                def _(sy, dc=dc):
                    sy.dma_start(out=tx[:, 0:TOWN],
                                 in_=xsrc[dc * 128:(dc + 1) * 128, :]
                                 ).then_inc(S("lr"), 16)
                    bump("lr", 16)
                @blk.vector
                def _(v, dc=dc):
                    v.wait_ge(S("lr"), cur("lr"))
                    v.tensor_tensor(out=res[:, dc, :], in0=res[:, dc, :],
                                    in1=tx[:, 0:TOWN], op=ALU.add)

    def emit_ln(tag):
        for f in range(TOWN // 512):
            c0 = f * 512
            b1 = cur("lna"); b2 = cur("lnb"); b3 = cur("lnc")
            with nc.Block() as blk:
                @blk.tensor
                def _(te, c0=c0):
                    for dc in range(DC):
                        inst = te.matmul(pm[0:1, :], onesc[:, :],
                                         res[:, dc, c0:c0 + 512],
                                         start=(dc == 0), stop=(dc == DC - 1))
                        if dc == DC - 1:
                            inst.then_inc(S("lna"), 1)
                    bump("lna")
                @blk.vector
                def _(v, c0=c0, b1=b1, b4=cur("lnd0")):
                    for dc in range(4):
                        inst = v.tensor_tensor(out=tx[:, dc * 512:(dc + 1) * 512],
                                               in0=res[:, dc, c0:c0 + 512],
                                               in1=res[:, dc, c0:c0 + 512],
                                               op=ALU.mult)
                        if dc == 3:
                            inst.then_inc(S("lnb"), 1)
                    bump("lnb")
                    v.wait_ge(S("lna"), b1 + 1)
                    v.tensor_copy(stat[:, 0, :], pm[0:1, :]).then_inc(S("lnc"), 1)
                    bump("lnc")
                    v.wait_ge(S("lnd0"), b4 + 1)
                    for dc in range(4):
                        inst = v.tensor_tensor(out=tx[:, dc * 512:(dc + 1) * 512],
                                               in0=res[:, dc + 4, c0:c0 + 512],
                                               in1=res[:, dc + 4, c0:c0 + 512],
                                               op=ALU.mult)
                        if dc == 3:
                            inst.then_inc(S("lnb"), 1)
                    bump("lnb")
                @blk.tensor
                def _(te, b2=b2, b3=b3):
                    te.wait_ge(S("lnb"), b2 + 1)
                    te.wait_ge(S("lnc"), b3 + 1)
                    for dc in range(4):
                        inst = te.matmul(pm[0:1, :], onesc[:, :],
                                         tx[:, dc * 512:(dc + 1) * 512],
                                         start=(dc == 0), stop=False)
                        if dc == 3:
                            inst.then_inc(S("lnd0"), 1)
                    bump("lnd0")
                    te.wait_ge(S("lnb"), b2 + 2)
                    for dc in range(4):
                        inst = te.matmul(pm[0:1, :], onesc[:, :],
                                         tx[:, dc * 512:(dc + 1) * 512],
                                         start=False, stop=(dc == 3))
                        if dc == 3:
                            inst.then_inc(S("lnd"), 1)
                    bump("lnd")
                @blk.vector
                def _(v):
                    v.wait_ge(S("lnd"), cur("lnd"))
                    v.tensor_copy(stat[:, 1, :], pm[0:1, :])
                    v.drain()
                    v.tensor_scalar(stat[:, 2, :], stat[:, 0, :],
                                    1.0 / D, None, op0=ALU.mult)
                    v.tensor_scalar(stat[:, 1, :], stat[:, 1, :],
                                    1.0 / D, None, op0=ALU.mult)
                    v.drain()
                    v.tensor_tensor(out=stat[:, 0, :], in0=stat[:, 2, :],
                                    in1=stat[:, 2, :], op=ALU.mult)
                    v.drain()
                    v.tensor_tensor(out=stat[:, 1, :], in0=stat[:, 1, :],
                                    in1=stat[:, 0, :], op=ALU.subtract)
                    v.drain()
                    v.tensor_scalar(stat[:, 1, :], stat[:, 1, :],
                                    1e-5, None, op0=ALU.add).then_inc(S("lne"), 1)
                    bump("lne")
                @blk.scalar
                def _(sc):
                    sc.wait_ge(S("lne"), cur("lne"))
                    sc.activation(out=stat[:, 0, :], in_=stat[:, 1, :],
                                  func=ACT.Ln, scale=1.0)
                    sc.drain()
                    sc.activation(out=stat[:, 3, :], in_=stat[:, 0, :],
                                  func=ACT.Exp, scale=-0.5).then_inc(S("lnf"), 1)
                    bump("lnf")
                @blk.tensor
                def _(te):
                    te.wait_ge(S("lnf"), cur("lnf"))
                    te.matmul(pa[:, 0:512], ones1[:, :], stat[:, 2, :],
                              start=True, stop=True)
                    te.matmul(pm[:, :], ones1[:, :], stat[:, 3, :],
                              start=True, stop=True).then_inc(S("lng"), 1)
                    bump("lng")
                @blk.vector
                def _(v, c0=c0):
                    v.wait_ge(S("lng"), cur("lng"))
                    for dc in range(DC):
                        v.tensor_tensor(out=tx[:, 0:512],
                                        in0=res[:, dc, c0:c0 + 512],
                                        in1=pa[:, 0:512], op=ALU.subtract)
                        v.drain()
                        v.tensor_tensor(out=res[:, dc, c0:c0 + 512],
                                        in0=tx[:, 0:512],
                                        in1=pm[:, :], op=ALU.mult)
                        v.drain()

    emit_residual(xq)
    emit_ln("n1")

    xn16 = alloc("xn16", [128, DC, TOWN], BF16)
    h16 = alloc("h16", [128, DC, TOWN], BF16)
    tf = alloc("tf", [128, 1024], F32)
    with nc.Block() as blk:
        @blk.vector
        def _(v):
            for dc in range(DC):
                v.tensor_copy(xn16[:, dc, :], res[:, dc, :])

    emit_proj(w1d, 0, 8,
              lambda dcc, c0: xn16[:, dcc, c0:c0 + 512], 2,
              lambda ec, half: h16[:, ec, half * 1024:(half + 1) * 1024],
              "f1", evac="scalar", act=ACT.Copy, scale=1.0)
    emit_proj(w2d, 0, 8,
              lambda dcc, c0: h16[:, dcc, c0:c0 + 512], 2,
              lambda ec, half: xn16[:, ec, half * 1024:(half + 1) * 1024],
              "f2", evac="scalar", act=ACT.Copy, scale=1.0)
    bw3 = cur("wld"); bo3 = cur("wok")
    bm3 = cur("mm"); be3 = cur("ev"); ba3 = cur("fadd")
    with nc.Block() as blk:
        @blk.sync
        def _(sy):
            for i in range(8):
                if i >= 1:
                    sy.wait_ge(S("wok"), bo3 + i)
                for dcc in range(DC):
                    sy.dma_start(
                        out=wf[:, dcc * 128:(dcc + 1) * 128],
                        in_=w3d[dcc * 128:(dcc + 1) * 128,
                                i * 128:(i + 1) * 128],
                    ).then_inc(S("wld"), 16)
                bump("wld", 16 * DC)
        def emit_cast(v, i):
            v.wait_ge(S("wld"), bw3 + 16 * DC * (i + 1))
            if i >= 2:
                v.wait_ge(S("mm"), bm3 + (i - 1) * 2)
            v.tensor_copy(w16[i % 2][:, :], wf[:, :]).then_inc(S("wok"), 1)
            bump("wok")

        def emit_add(v, m):
            i, half = m // 2, m % 2
            v.wait_ge(S("ev"), be3 + m + 1)
            v.tensor_tensor(out=res[:, i, half * 1024:(half + 1) * 1024],
                            in0=res[:, i, half * 1024:(half + 1) * 1024],
                            in1=tf[:, :], op=ALU.add).then_inc(S("fadd"), 1)
            bump("fadd")

        @blk.vector
        def _(v):
            emit_cast(v, 0)
            emit_cast(v, 1)
            for i in range(2, 8):
                emit_add(v, 2 * i - 4)
                emit_add(v, 2 * i - 3)
                emit_cast(v, i)
            for m in range(12, 16):
                emit_add(v, m)
        @blk.tensor
        def _(te):
            m = 0
            for i in range(8):
                te.wait_ge(S("wok"), bo3 + i + 1)
                for half in range(2):
                    psu = pa if m % 2 == 0 else pb
                    if m >= 2:
                        te.wait_ge(S("ev"), be3 + m - 1)
                    for dcc in range(DC):
                        for f in range(2):
                            c0 = half * 1024 + f * 512
                            inst = te.matmul(psu[:, f * 512:(f + 1) * 512],
                                             w16[i % 2][:, dcc * 128:(dcc + 1) * 128],
                                             xn16[:, dcc, c0:c0 + 512],
                                             start=(dcc == 0), stop=(dcc == DC - 1))
                            if dcc == DC - 1 and f == 1:
                                inst.then_inc(S("mm"), 1)
                    bump("mm")
                    m += 1
        @blk.scalar
        def _(sc):
            m = 0
            for i in range(8):
                for half in range(2):
                    psu = pa if m % 2 == 0 else pb
                    sc.wait_ge(S("mm"), bm3 + m + 1)
                    if m >= 1:
                        sc.wait_ge(S("fadd"), ba3 + m)
                    sc.activation(out=tf[:, :], in_=psu[:, :], func=ACT.Tanh,
                                  scale=1.0,
                                  bias=fbb[:, i:i + 1]).then_inc(S("ev"), 1)
                    bump("ev")
                    m += 1
    free("tf", "h16", "xn16")
    emit_ln("n2")

    for tt in range(QT):
        b_tr = cur("otr"); b_te = cur("ote")
        with nc.Block() as blk:
            @blk.tensor
            def _(te, tt=tt):
                for dc in range(DC):
                    inst = te.transpose(pa[:, dc * 128:(dc + 1) * 128],
                                        res[:, dc, tt * 128:(tt + 1) * 128],
                                        eye32[:, :])
                    if dc == DC - 1:
                        inst.then_inc(S("otr"), 1)
                bump("otr")
            @blk.vector
            def _(v, b_tr=b_tr, tt=tt):
                v.wait_ge(S("otr"), b_tr + 1)
                v.wait_ge(S("st"), 16 * tt)
                v.tensor_copy(tx[:, 0:1024], pa[:, :]).then_inc(S("ote"), 1)
                bump("ote")
            @blk.sync
            def _(sy, tt=tt, b_te=b_te):
                sy.wait_ge(S("ote"), b_te + 1)
                sy.dma_start(out=out[tt * 128:(tt + 1) * 128, :],
                             in_=tx[:, 0:1024]).then_inc(S("st"), 16)
                bump("st", 16)

    with nc.Block() as blk:
        @blk.sync
        def _(sy):
            sy.wait_ge(S("st"), cur("st"))

    for n in reversed(list(live)):
        live.pop(n).__exit__(None, None, None)
    for name in list(sems):
        sems.pop(name)[0].__exit__(None, None, None)
    return nc


def _finish_stub(nc, out, src16, S, bump, cur, live, sems):
    with (nc.sbuf_tensor("stub", [128, 1024], F32) as stub,
          nc.Block() as blk):
        @blk.vector
        def _(v):
            for dc in range(DC):
                v.wait_ge(S("st"), 16 * dc)
                v.tensor_copy(stub[:, :], src16[:, dc, 0:1024]).then_inc(S("stub"), 1)
                bump("stub")
        @blk.sync
        def _(sy):
            for dc in range(DC):
                sy.wait_ge(S("stub"), dc + 1)
                sy.dma_start(out=out[dc * 128:(dc + 1) * 128, :],
                             in_=stub[:, :]).then_inc(S("st"), 16)
                bump("st", 16)
            sy.wait_ge(S("st"), cur("st"))
    for n in reversed(list(live)):
        live.pop(n).__exit__(None, None, None)
    for name in list(sems):
        sems.pop(name)[0].__exit__(None, None, None)
    return nc


_NC_CACHE = None

def kernel(L0_attn_x, L0_attn_W_qkv, L0_attn_W_out, L0_attn_b_out,
           L0_ffn_weight, L0_ffn_aux_weight1, L0_ffn_aux_weight2, L0_ffn_bias):
    global _NC_CACHE
    x = np.asarray(L0_attn_x, dtype=np.float32)
    wqkv = np.asarray(L0_attn_W_qkv, dtype=np.float32)
    wout = np.asarray(L0_attn_W_out, dtype=np.float32)
    b_out = np.asarray(L0_attn_b_out, dtype=np.float32)
    w1 = np.asarray(L0_ffn_weight, dtype=np.float32)
    w2 = np.asarray(L0_ffn_aux_weight1, dtype=np.float32)
    w3 = np.asarray(L0_ffn_aux_weight2, dtype=np.float32)
    fb = np.asarray(L0_ffn_bias, dtype=np.float32)

    x_fm = np.ascontiguousarray(np.transpose(x, (0, 2, 1)))
    wqkvT = np.ascontiguousarray(wqkv.T)
    woutT = np.ascontiguousarray(wout.T)
    eye = np.eye(128, dtype=np.float32)
    boutr = np.ascontiguousarray(b_out.reshape(DC, 128))
    fbr = np.ascontiguousarray(fb.reshape(DC, 128))

    if _NC_CACHE is None:
        _NC_CACHE = build_nc()
    nc = _NC_CACHE

    in_maps = []
    for c in range(8):
        b, half = c // 2, c % 2
        in_maps.append({
            "xq": np.ascontiguousarray(x_fm[b][:, half * TOWN:(half + 1) * TOWN]),
            "xfull": x_fm[b],
            "wqkvT": wqkvT,
            "woutT": woutT,
            "boutd": boutr,
            "w1d": w1, "w2d": w2, "w3d": w3,
            "fbiasd": fbr,
            "eyed": eye,
        })
    res = run_bass_kernel_spmd(nc, in_maps, core_ids=list(range(8)))
    outp = np.empty((B, T, D), dtype=np.float32)
    for c in range(8):
        b, half = c // 2, c % 2
        outp[b, half * TOWN:(half + 1) * TOWN, :] = res.results[c]["out"]
    return outp
